# version 14
# baseline (speedup 1.0000x reference)
import numpy as np
import ml_dtypes
from contextlib import ExitStack

import concourse.bass as bass
import concourse.mybir as mybir
from concourse.bass_utils import run_bass_kernel_spmd

F32 = mybir.dt.float32
BF16 = mybir.dt.bfloat16
I32 = mybir.dt.int32
AF = mybir.ActivationFunctionType
ALU = mybir.AluOpType
AX = mybir.AxisListType

D = 4096
T = 1024
TC = 8
KC = 32
JN = 16
NS = 256
HALF = 2048
NCORES = 8

ENGINES = ("sync", "tensor", "vector", "scalar", "gpsimd")


class Rec:

    COMPUTE_SEMS = ("sT", "sV", "sA", "sP")

    def __init__(self):
        self.streams = {e: [] for e in ENGINES}
        self.counts = {}
        self.last_compute = {e: None for e in ENGINES}
        self.last_dma = {}

    def emit(self, eng, fn, waits=(), inc=None, amt=1):
        wl = {}
        extra = [self.last_compute[eng]]
        if inc is not None and inc not in self.COMPUTE_SEMS:
            extra.append(self.last_dma.get(inc))
        for w in list(waits) + extra:
            if w is None:
                continue
            s, v, pe = w
            if wl.get(s, -1) < v:
                wl[s] = v
        tok = None
        if inc is not None:
            self.counts[inc] = self.counts.get(inc, 0) + amt
            tok = (inc, self.counts[inc], eng)
            if inc in self.COMPUTE_SEMS:
                self.last_compute[eng] = tok
            else:
                self.last_dma[inc] = tok
        self.streams[eng].append((fn, wl, tok, amt))
        return tok


def build_program():
    nc = bass.Bass("TRN2", target_bir_lowering=False)

    i_x = nc.dram_tensor("x", [T, D], F32, kind="ExternalInput").ap()
    i_mw = nc.dram_tensor("mw", [JN, 128, 8192], I32, kind="ExternalInput").ap()
    i_bw = nc.dram_tensor("bw", [JN, 128, 8192], F32, kind="ExternalInput").ap()
    i_bu = nc.dram_tensor("bu", [JN, 128, 8192], F32, kind="ExternalInput").ap()
    i_ag = nc.dram_tensor("agh", [128, 32 * 36], F32, kind="ExternalInput").ap()
    i_bb = nc.dram_tensor("bb", [32, D], F32, kind="ExternalInput").ap()
    i_bias = nc.dram_tensor("bias", [1, D], F32, kind="ExternalInput").ap()
    i_eb = nc.dram_tensor("ebias", [1, 4], F32, kind="ExternalInput").ap()
    i_id = nc.dram_tensor("ident", [128, 128], BF16, kind="ExternalInput").ap()
    i_g = nc.dram_tensor("gmat", [128, 128], BF16, kind="ExternalInput").ap()
    i_on = nc.dram_tensor("onescol", [1, 128], BF16, kind="ExternalInput").ap()
    d_xmid = nc.dram_tensor("xmid", [T, D], F32, kind="Internal").ap()
    o_y = nc.dram_tensor("y", [T, D], F32, kind="ExternalOutput").ap()

    rec = Rec()

    with ExitStack() as st:
        sb = lambda n, s, d: st.enter_context(nc.sbuf_tensor(n, s, d))
        psum = lambda n, s, d: st.enter_context(nc.psum_tensor(n, s, d))

        fm = sb("fm", [128, KC * 1024], BF16)
        mo = sb("mo", [128, TC * 4096], BF16)
        wmb = [sb("wmb0", [128, 8192], BF16),
               sb("wmb1", [128, 8192], BF16)]
        mring = [sb("mring0", [128, 1024], I32),
                 sb("mring1", [128, 1024], I32)]
        xst = [sb("xst0", [128, HALF], F32),
               sb("xst1", [128, HALF], F32)]
        xbst = sb("xbst", [128, D], BF16)
        junk = sb("junk", [128, HALF], BF16)
        agb = sb("agb", [128, 32 * 36], BF16)
        psj_i32 = sb("psj_i32", [128, 2 * 256], I32)
        psj_bf = sb("psj_bf", [128, 2 * 256], BF16)
        bmpj = sb("bmpj", [128, 2 * 256], BF16)
        zfm = sb("zfm", [32, TC * 128], BF16)
        zpb = sb("zpb", [128, 32], BF16)
        acc_s = sb("acc_s", [128, 128], F32)
        acc_q = sb("acc_q", [128, 128], F32)
        sums_h = sb("sums_h", [128, 4], F32)
        ss_h = sb("ss_h", [128, 4], F32)
        lg = sb("lg", [128, 4], F32)
        ohm = sb("ohm", [128, 4], F32)
        cmp9 = sb("cmp9", [128, 12], F32)
        ebias_b = sb("ebias_b", [128, 4], F32)
        identb = sb("identb", [128, 128], BF16)
        gmatb = sb("gmatb", [128, 128], BF16)
        onescol = sb("onescol_s", [1, 128], BF16)
        epsv = sb("epsv", [128, 1], F32)
        c_inv = sb("c_inv", [128, 1], F32)
        c_neg1 = sb("c_neg1", [128, 1], F32)
        c_zero = sb("c_zero", [128, 1], F32)
        c_one = sb("c_one", [128, 1], F32)
        stat = {nm: sb("st_" + nm, [128, 7], F32) for nm in ("a", "b", "c", "d")}
        rq4 = sb("rq4", [128, 16], F32)
        rq6 = sb("rq6", [128, 16], F32)
        rqP2 = sb("rqP2", [128, 4], F32)
        rqP4 = sb("rqP4", [128, 4], F32)

        pm = [psum(f"pm{b}", [128, NS], F32) for b in range(4)]
        pt = [psum(f"pt{b}", [128, 512], BF16) for b in range(2)]
        pk = [psum(f"pk{b}", [128, 256], F32) for b in range(2)]

        sems = {n: st.enter_context(nc.semaphore(n))
                for n in ("dC", "dM0", "dM1", "dW0", "dW1", "dX0", "dX1",
                          "dX2", "dX3", "dO0", "dO1", "dO2", "dO3",
                          "dP0", "dP1", "dP2", "dP3", "dP4", "dP5", "dP6", "dP7",
                          "dE", "dB", "sT", "sV", "sA", "sP")}

        bb_alias = [m[:, :].bitcast(BF16) for m in mring]
        bias_v = xst[0][0:1, :].bitcast(BF16)
        wmb0f = wmb[0][:, :].bitcast(F32)
        wmb1f = wmb[1][:, :].bitcast(F32)
        mof = mo[:, :].bitcast(F32)
        fmf = fm[:, :].bitcast(F32)
        xslots = [(xst[0][:, :], "dX0"), (xst[1][:, :], "dX1")]
        w0slots = [(wmb0f[:, 0:HALF], "dX2"), (wmb0f[:, HALF:2 * HALF], "dX3")]
        w1slots = [(wmb1f[:, 0:HALF], "dX2"), (wmb1f[:, HALF:2 * HALF], "dX3")]
        fm_v = fm[:, :].rearrange("p (kc t) -> p kc t", t=1024)

        E = rec.emit
        _store_ctr = [0]

        def store_sem():
            n = "dO%d" % (_store_ctr[0] % 4)
            _store_ctr[0] += 1
            return n

        E("sync", lambda e: e.dma_start(out=identb[:, :], in_=i_id[:, :]), inc="dC", amt=16)
        E("sync", lambda e: e.dma_start(out=gmatb[:, :], in_=i_g[:, :]), inc="dC", amt=16)
        t_consts = E("sync", lambda e: e.dma_start(out=onescol[:, :], in_=i_on[:, :]), inc="dC", amt=16)
        t_eb = E("scalar", lambda e: e.dma_start(out=ebias_b[:, :], in_=i_eb.broadcast_to([128, 4])), inc="dE", amt=16)
        t_ag = E("gpsimd", lambda e: e.dma_start(out=agb[:, :], in_=i_ag[:, :]), inc="dB", amt=16)
        E("vector", lambda e: e.memset(epsv[:, :], 1e-5), inc="sV")
        E("vector", lambda e: e.memset(c_inv[:, :], 1.0 / D), inc="sV")
        E("vector", lambda e: e.memset(c_neg1[:, :], -1.0), inc="sV")
        E("vector", lambda e: e.memset(c_zero[:, :], 0.0), inc="sV")
        t_const = E("vector", lambda e: e.memset(c_one[:, :], 1.0), inc="sV")

        def stats_chain(sk, t_sums, t_ss, sl=(0, 1), r_ap=None, q_ap=None):
            s = stat[sk]
            r_ap = r_ap if r_ap is not None else s[:, 5:6]
            q_ap = q_ap if q_ap is not None else s[:, 6:7]
            E("vector", lambda e: e.tensor_add(s[:, 1:2], sums_h[:, sl[0]:sl[0] + 1], sums_h[:, sl[1]:sl[1] + 1]),
              waits=[t_sums], inc="sV")
            E("vector", lambda e: e.tensor_scalar(s[:, 0:1], s[:, 1:2], c_inv[:, 0:1], None, ALU.mult), inc="sV")
            E("vector", lambda e: e.tensor_add(s[:, 2:3], ss_h[:, sl[0]:sl[0] + 1], ss_h[:, sl[1]:sl[1] + 1]),
              waits=[t_ss], inc="sV")
            E("vector", lambda e: e.tensor_scalar(s[:, 2:3], s[:, 2:3], c_inv[:, 0:1], None, ALU.mult), inc="sV")
            E("vector", lambda e: e.tensor_mul(s[:, 3:4], s[:, 0:1], s[:, 0:1]), inc="sV")
            t_var = E("vector", lambda e: e.tensor_sub(s[:, 3:4], s[:, 2:3], s[:, 3:4]), inc="sV")
            t_sd = E("scalar", lambda e: e.activation(s[:, 4:5], s[:, 3:4], AF.Sqrt,
                                                      bias=epsv[:, 0:1], scale=1.0),
                     waits=[t_var, t_const], inc="sA")
            t_r = E("vector", lambda e, r_ap=r_ap: e.reciprocal(r_ap, s[:, 4:5]), waits=[t_sd], inc="sV")
            E("vector", lambda e, r_ap=r_ap: e.tensor_mul(s[:, 6:7], s[:, 0:1], r_ap), inc="sV")
            t_q = E("vector", lambda e, q_ap=q_ap: e.tensor_scalar(q_ap, s[:, 6:7], c_neg1[:, 0:1], None, ALU.mult), inc="sV")
            return t_r, t_q

        def stats_chain_acc(sk, i, t_ready, r_ap=None, q_ap=None):
            s = stat[sk]
            r_ap = r_ap if r_ap is not None else s[:, 5:6]
            q_ap = q_ap if q_ap is not None else s[:, 6:7]
            t1 = E("vector", lambda e: e.reduce_sum(s[:, 1:2], acc_s[:, i * 16:(i + 1) * 16], axis=AX.X),
                   waits=[t_ready], inc="sV")
            E("vector", lambda e: e.tensor_scalar(s[:, 0:1], s[:, 1:2], c_inv[:, 0:1], None, ALU.mult), inc="sV")
            t2 = E("vector", lambda e: e.reduce_sum(s[:, 2:3], acc_q[:, i * 16:(i + 1) * 16], axis=AX.X),
                   waits=[t_ready], inc="sV")
            E("vector", lambda e: e.tensor_scalar(s[:, 2:3], s[:, 2:3], c_inv[:, 0:1], None, ALU.mult), inc="sV")
            E("vector", lambda e: e.tensor_mul(s[:, 3:4], s[:, 0:1], s[:, 0:1]), inc="sV")
            t_var = E("vector", lambda e: e.tensor_sub(s[:, 3:4], s[:, 2:3], s[:, 3:4]), inc="sV")
            t_sd = E("scalar", lambda e: e.activation(s[:, 4:5], s[:, 3:4], AF.Sqrt,
                                                      bias=epsv[:, 0:1], scale=1.0),
                     waits=[t_var], inc="sA")
            t_r = E("vector", lambda e, r_ap=r_ap: e.reciprocal(r_ap, s[:, 4:5]), waits=[t_sd], inc="sV")
            E("vector", lambda e, r_ap=r_ap: e.tensor_mul(s[:, 6:7], s[:, 0:1], r_ap), inc="sV")
            t_q = E("vector", lambda e, q_ap=q_ap: e.tensor_scalar(q_ap, s[:, 6:7], c_neg1[:, 0:1], None, ALU.mult), inc="sV")
            return t_r, t_q, (t1, t2)

        tr_state = {"seq": 0, "evtok": {}}

        def transpose_chunk(i, t_apply, src=None, evac_eng="scalar"):
            src = src if src is not None else xbst
            toks = []
            for g in range(8):
                seq = tr_state["seq"]
                slot = pt[seq % 2]
                w = [t_apply, tr_state["evtok"].get(seq - 2), t_consts]
                t_tr = None
                for u in range(4):
                    kc = g * 4 + u
                    def f(e, kc=kc, u=u, slot=slot, src=src):
                        return e.transpose(slot[0:128, u * 128:(u + 1) * 128],
                                           src[:, kc * 128:(kc + 1) * 128], identb[:, :])
                    t_tr = E("tensor", f, waits=w if u in (0, 3) else (),
                             inc="sT" if u == 3 else None)
                def fe(e, g=g, i=i, slot=slot):
                    fn = e.copy if hasattr(e, "copy") else e.tensor_copy
                    return fn(
                        out=fm_v[:, g * 4:(g + 1) * 4, i * 128:(i + 1) * 128],
                        in_=slot[0:128, 0:512].rearrange("p (a b) -> p a b", b=128))
                t_ev = E(evac_eng, fe, waits=[t_tr],
                         inc="sA" if evac_eng == "scalar" else "sV")
                tr_state["evtok"][seq] = t_ev
                tr_state["seq"] += 1
                toks.append(t_ev)
            return toks

        apply_tok = {}
        sqh_tok = {}
        xdma_tok = {}
        tr_done = {}
        def moq(c):
            return [(mof[:, c * 4096:c * 4096 + HALF], None),
                    (mof[:, c * 4096 + HALF:(c + 1) * 4096], None)]
        p2_slotmap = {0: moq(0), 1: moq(1), 2: moq(2), 3: moq(3),
                      4: xslots, 5: w1slots, 6: moq(0), 7: moq(1)}
        p2_sems = ["dP0", "dP1", "dP2", "dP3", "dP4", "dP5", "dP6", "dP7"]

        def rec_p2_chunk(c):
            slots = p2_slotmap[c]
            sk = "a" if c % 2 == 0 else "b"
            pr = c % 2
            half_toks = []
            for h in range(2):
                ap_slot = slots[h][0]
                semn = p2_sems[(2 * c + h) % 8]
                sl = pr * 2 + h
                w = ([apply_tok[c - 6], sqh_tok[(c - 6, h)]] if c >= 6 else [])
                def fd(e, c=c, h=h, ap_slot=ap_slot):
                    return e.dma_start(out=ap_slot,
                                      in_=i_x[c * 128:(c + 1) * 128, h * HALF:(h + 1) * HALF])
                t_d = E("gpsimd", fd, waits=w, inc=semn, amt=16)
                xdma_tok[(c, h)] = t_d
                def fs(e, ap_slot=ap_slot, sl=sl):
                    return e.reduce_sum(sums_h[:, sl:sl + 1], ap_slot, axis=AX.X)
                t_s = E("vector", fs, waits=[t_d], inc="sV")
                def fq(e, ap_slot=ap_slot, sl=sl):
                    return e.activation(junk[:, :], ap_slot,
                                        AF.Square, accum_out=ss_h[:, sl:sl + 1])
                t_q = E("scalar", fq, waits=[t_d, apply_tok.get(c - 2)], inc="sA")
                sqh_tok[(c, h)] = t_q
                half_toks.append((t_s, t_q))
            t_r, t_qq = stats_chain(sk, half_toks[1][0], half_toks[1][1],
                                    sl=(pr * 2, pr * 2 + 1),
                                    r_ap=rqP2[:, pr * 2:pr * 2 + 1],
                                    q_ap=rqP2[:, pr * 2 + 1:pr * 2 + 2])
            ap = None
            for h in range(2):
                def fa(e, h=h, pr=pr, ap_slot=slots[h][0]):
                    return e.tensor_scalar(
                        out=xbst[:, h * HALF:(h + 1) * HALF],
                        in0=ap_slot,
                        scalar1=rqP2[:, pr * 2:pr * 2 + 1],
                        scalar2=rqP2[:, pr * 2 + 1:pr * 2 + 2],
                        op0=ALU.mult, op1=ALU.add)
                ap = E("vector", fa, waits=[t_r, t_qq, half_toks[h][1],
                                            (tr_done.get(c - 1) or [None])[-1]], inc="sV")
            apply_tok[c] = ap
            tr_done[c] = transpose_chunk(c, ap, evac_eng="vector")

        seg_tok = {}
        cast_half_tok = {}
        cvt_tok = {}
        g_tok = {}
        bm_tok = {}
        wm_tok = {}
        cast_tok = {}
        main_tok = {}
        ev_tok = {}
        last_seg = {"t": None}
        last_mdma = {}

        def rec_mask_pieces(j, rng=range(8)):
            for pc in rng:
                idx = j * 8 + pc
                def fd(e, j=j, pc=pc, idx=idx):
                    return e.dma_start(out=mring[idx % 2][:, :],
                                      in_=i_mw[j, :, pc * 1024:(pc + 1) * 1024])
                t_d = E("sync", fd, waits=[seg_tok.get(((idx - 2) // 8, (idx - 2) % 8))],
                        inc="dM%d" % (idx % 2), amt=16)
                def fs(e, j=j, pc=pc, idx=idx):
                    with nc.allow_low_precision(reason="int32 block-sum of mask ints is exact"):
                        return e.tensor_reduce(
                            out=psj_i32[:, (j % 2) * 256 + pc * 32:(j % 2) * 256 + (pc + 1) * 32],
                            in_=mring[idx % 2][:, :].rearrange("p (g w) -> p g w", w=32),
                            axis=AX.X, op=ALU.add)
                t_s = E("vector", fs, waits=[t_d, cvt_tok.get(j - 2)], inc="sV")
                seg_tok[(j, pc)] = t_s
                last_seg["t"] = t_s
                last_mdma[idx % 2] = t_d

        def rec_mask_cvt(j):
            def fc(e, j=j):
                return e.tensor_copy(out=psj_bf[:, (j % 2) * 256:(j % 2 + 1) * 256],
                                     in_=psj_i32[:, (j % 2) * 256:(j % 2 + 1) * 256])
            cvt_tok[j] = E("vector", fc, waits=[seg_tok[(j, 7)], g_tok.get(j - 2)], inc="sV")

        def rec_mask_gbm(j):
            def fg(e, j=j):
                return e.matmul(pk[j % 2][0:128, 0:256], gmatb[:, :],
                                psj_bf[:, (j % 2) * 256:(j % 2 + 1) * 256],
                                start=True, stop=True)
            t_g = E("tensor", fg, waits=[cvt_tok[j], t_consts, bm_tok.get(j - 2)], inc="sT")
            g_tok[j] = t_g
            def fb(e, j=j):
                return e.tensor_single_scalar(out=bmpj[:, (j % 2) * 256:(j % 2 + 1) * 256],
                                              in_=pk[j % 2][0:128, 0:256],
                                              scalar=c_zero[:, 0:1], op=ALU.is_gt)
            bm_tok[j] = E("vector", fb, waits=[t_g, t_const, wm_tok.get(j - 2)], inc="sV")

        def rec_weight_casts(j, hbm, extra_waits=()):
            toks = []
            for hh in range(2):
                def fd(e, j=j, hh=hh, hbm=hbm):
                    return e.dma_start(out=wmb[j % 2][:, hh * 4096:(hh + 1) * 4096],
                                      in_=hbm[j, :, hh * 4096:(hh + 1) * 4096])
                toks.append(E("gpsimd", fd,
                              waits=[main_tok.get((j - 2, TC - 1)),
                                     wm_tok.get(j - 2)] + list(extra_waits) if hh == 0 else [],
                              inc="dW%d" % (j % 2), amt=16))
            cast_tok[j] = toks[1]
            cast_half_tok[j] = toks

        def rec_mask_tt_half(j, hh):
            def ft(e, j=j, hh=hh):
                return e.tensor_tensor(
                    out=wmb[j % 2][:, hh * 4096:(hh + 1) * 4096]
                        .rearrange("p (kc nb w) -> p kc nb w", kc=16, w=32),
                    in0=wmb[j % 2][:, hh * 4096:(hh + 1) * 4096]
                        .rearrange("p (kc nb w) -> p kc nb w", kc=16, w=32),
                    in1=bmpj[:, (j % 2) * 256 + hh * 128:(j % 2) * 256 + (hh + 1) * 128]
                        .rearrange("p (kc nb) -> p kc nb", kc=16)
                        .unsqueeze(3).broadcast_to([128, 16, 8, 32]),
                    op=ALU.mult)
            tok = E("vector", ft, waits=[cast_half_tok[j][hh], bm_tok[j]], inc="sV")
            if hh == 1:
                wm_tok[j] = tok
            return tok

        def rec_mains(j, phase, t_bias=None, t_bb=None, zfm_toks=None, mo_free=None,
                      acc_free=None, post_i_cb=None):
            for i in range(TC):
                seq = (0 if phase == 3 else JN * TC) + j * TC + i
                bank = pm[seq % 4]
                w = [wm_tok.get(j) if phase == 3 else cast_tok[j],
                     tr_done[i][-1] if phase == 3 else None,
                     ev_tok.get(seq - 4),
                     t_bias, t_bb,
                     zfm_toks[i] if zfm_toks else None,
                     t_consts]
                mm = None
                for k in range(KC):
                    def fmm(e, j=j, i=i, k=k, bank=bank):
                        return e.matmul(bank[0:128, 0:NS],
                                        fm[:, k * 1024 + i * 128:k * 1024 + i * 128 + 128],
                                        wmb[j % 2][:, k * NS:(k + 1) * NS],
                                        start=(k == 0), stop=False)
                    E("tensor", fmm, waits=w if k == 0 else ())
                if phase == 3:
                    def fr1(e, j=j, bank=bank):
                        return e.matmul(bank[0:128, 0:NS], onescol[0:1, 0:128],
                                        bias_v[0:1, j * NS:(j + 1) * NS],
                                        start=False, stop=True)
                    t_m = E("tensor", fr1, waits=w, inc="sT")
                else:
                    def fdl(e, j=j, i=i, bank=bank):
                        ba = bb_alias[j // 8]
                        return e.matmul(bank[0:128, 0:NS], zfm[0:32, i * 128:(i + 1) * 128],
                                        ba[0:32, (j % 8) * NS:(j % 8 + 1) * NS],
                                        start=False, stop=True)
                    t_m = E("tensor", fdl, waits=w, inc="sT")
                main_tok[(j, i)] = t_m
                def fe1(e, j=j, i=i, bank=bank):
                    return e.activation(out=mo[:, i * 4096 + j * NS:i * 4096 + (j + 1) * NS],
                                        in_=bank[0:128, 0:NS], func=AF.Identity,
                                        accum_out=acc_s[:, i * 16 + j:i * 16 + j + 1])
                mw = mo_free[i] if mo_free else None
                mw = mw if isinstance(mw, list) else [mw]
                E("scalar", fe1, waits=[t_m, acc_free[i] if acc_free else None] + mw,
                  inc="sA")
                def fe2(e, j=j, i=i, bank=bank):
                    return e.activation(out=junk[:, 0:NS], in_=bank[0:128, 0:NS],
                                        func=AF.Square,
                                        accum_out=acc_q[:, i * 16 + j:i * 16 + j + 1])
                ev_tok[seq] = E("scalar", fe2, inc="sA")
                if post_i_cb is not None:
                    post_i_cb(i, ev_tok[seq])

        rec_weight_casts(0, i_bw)
        rec_p2_chunk(0)
        rec_p2_chunk(1)
        rec_mask_pieces(0); rec_mask_cvt(0)
        rec_p2_chunk(2)
        rec_mask_gbm(0)
        rec_mask_tt_half(0, 0); rec_mask_tt_half(0, 1)
        rec_p2_chunk(3)
        rec_mask_pieces(1); rec_mask_cvt(1)
        rec_p2_chunk(4)
        rec_p2_chunk(5)
        rec_mask_gbm(1)
        rec_p2_chunk(6)
        rec_p2_chunk(7)
        fm_ready = tr_done[TC - 1][-1]
        def p2_guard(i):
            cs = [i // 2] + ([6] if i // 2 == 0 else ([7] if i // 2 == 1 else []))
            g = []
            for c in cs:
                g += [apply_tok[c], sqh_tok[(c, 0)], sqh_tok[(c, 1)],
                      xdma_tok[(c, 0)], xdma_tok[(c, 1)]]
            return g
        mo_guard = {i: p2_guard(i) for i in range(TC)}
        t_bias = E("gpsimd", lambda e: e.dma_start(out=bias_v[0:1, :], in_=i_bias[0:1, :]),
                   waits=[apply_tok[4], xdma_tok[(4, 0)],
                          sqh_tok[(4, 0)]], inc="dB", amt=16)
        wmb1_p2_free = [apply_tok[5], sqh_tok[(5, 0)], sqh_tok[(5, 1)],
                        xdma_tok[(5, 0)], xdma_tok[(5, 1)]]
        acc_read = {}
        rq4_tok = {}
        tmp4_tok = {}

        def p4_post_i(i, t_ev):
            t_r2, t_q2, _rd = stats_chain_acc("b", i, t_ev,
                                              r_ap=rq4[:, 2 * i:2 * i + 1],
                                              q_ap=rq4[:, 2 * i + 1:2 * i + 2])
            acc_read[i] = _rd[1]
            rq4_tok[i] = t_q2
            tmp4_tok[i] = E("vector", lambda e, i=i: e.tensor_scalar(
                out=mo[:, i * 4096:(i + 1) * 4096], in0=mo[:, i * 4096:(i + 1) * 4096],
                scalar1=rq4[:, 2 * i:2 * i + 1], scalar2=rq4[:, 2 * i + 1:2 * i + 2],
                op0=ALU.mult, op1=ALU.add),
                waits=[t_q2, t_ev], inc="sV")

        for j in range(JN):
            if j + 1 < JN:
                rec_weight_casts(j + 1, i_bw,
                                 extra_waits=wmb1_p2_free if j + 1 == 1 else ())
            if j + 2 < JN:
                rec_mask_pieces(j + 2, range(0, 4))
                rec_mask_tt_half(j + 1, 0)
                rec_mask_pieces(j + 2, range(4, 8))
                rec_mask_tt_half(j + 1, 1)
                rec_mask_cvt(j + 2)
            elif j + 1 < JN:
                rec_mask_tt_half(j + 1, 0)
                rec_mask_tt_half(j + 1, 1)
            rec_mains(j, phase=3, t_bias=t_bias,
                      mo_free=mo_guard if j == 0 else None,
                      post_i_cb=p4_post_i if j == JN - 1 else None)
            if j + 2 < JN:
                rec_mask_gbm(j + 2)
        attn_last = main_tok[(JN - 1, TC - 1)]

        E("gpsimd", lambda e: e.dma_start(out=bb_alias[0][0:32, :], in_=i_bb[:, 0:2048]),
          waits=[last_seg["t"], last_mdma.get(0)], inc="dB", amt=16)
        t_bb = E("gpsimd", lambda e: e.dma_start(out=bb_alias[1][0:32, :], in_=i_bb[:, 2048:4096]),
                 waits=[last_seg["t"], last_mdma.get(1)], inc="dB", amt=16)
        zfm_toks = {}
        r1_tok = {}
        apply4_tok = {}
        xm_tok = {}
        xd4_tok = {}
        z_tok = {}
        sq4_tok = {}
        pmap4 = [(i % 2) if i < TC - 2 else 1 for i in range(TC)]
        if TC >= 2:
            pmap4[0] = 0
        pairs4 = [w0slots, xslots]
        prev_same4 = {}
        lastu = {}
        for i in range(TC):
            prev_same4[i] = lastu.get(pmap4[i])
            lastu[pmap4[i]] = i
        for i in range(TC):
            pr = pmap4[i]
            slots = pairs4[pr]
            t_tmp = tmp4_tok[i]
            pv = prev_same4[i]
            if pv is None:
                xw0 = ([main_tok[(JN - 2, TC - 1)], wm_tok.get(JN - 2)] if pr == 0
                       else [attn_last, t_bias])
                xw = {0: xw0, 1: xw0}
            else:
                xw = {h: [apply4_tok[pv], sq4_tok[pv], xm_tok[(pv, h)]] for h in range(2)}
            td = {}
            for h in range(2):
                def fdx(e, i=i, h=h, ap_slot=slots[h][0]):
                    return e.dma_start(out=ap_slot,
                                      in_=i_x[i * 128:(i + 1) * 128, h * HALF:(h + 1) * HALF])
                td[h] = E("sync", fdx, waits=xw[h], inc=slots[h][1], amt=16)
                xd4_tok[(i, h)] = td[h]
            hts = []
            trh = {}
            for h in range(2):
                sl = pr * 2 + h
                def fr(e, i=i, h=h, ap_slot=slots[h][0], sl=sl):
                    return e.scalar_tensor_tensor(
                        out=ap_slot, in0=mo[:, i * 4096 + h * HALF:i * 4096 + (h + 1) * HALF],
                        scalar=c_one[:, 0:1], in1=ap_slot,
                        op0=ALU.mult, op1=ALU.add, accum_out=sums_h[:, sl:sl + 1])
                trh[h] = E("vector", fr, waits=[td[h], t_tmp, t_const], inc="sV")
                xm_tok[(i, h)] = E("scalar", lambda e, i=i, h=h, ap_slot=slots[h][0]: e.dma_start(
                    out=d_xmid[i * 128:(i + 1) * 128, h * HALF:(h + 1) * HALF], in_=ap_slot),
                    waits=[trh[h]], inc=store_sem(), amt=16)
                tq = E("scalar", lambda e, ap_slot=slots[h][0], sl=sl: e.activation(
                    junk[:, :], ap_slot, AF.Square,
                    accum_out=ss_h[:, sl:sl + 1]),
                    waits=[trh[h], apply4_tok.get(prev_same4[i])], inc="sA")
                hts.append((trh[h], tq))
            r1_tok[i] = trh[1]
            sq4_tok[i] = hts[1][1]
            t_r3, t_q3 = stats_chain("c" if pr == 0 else "d",
                                     hts[1][0], hts[1][1], sl=(pr * 2, pr * 2 + 1),
                                     r_ap=rqP4[:, pr * 2:pr * 2 + 1],
                                     q_ap=rqP4[:, pr * 2 + 1:pr * 2 + 2])
            xb4 = xbst if i == 0 else mo[:, (i - 1) * 4096:i * 4096]
            ap = None
            for h in range(2):
                ap = E("vector", lambda e, h=h, pr=pr, xb4=xb4, ap_slot=slots[h][0]: e.tensor_scalar(
                    out=xb4[:, h * HALF:(h + 1) * HALF], in0=ap_slot,
                    scalar1=rqP4[:, pr * 2:pr * 2 + 1],
                    scalar2=rqP4[:, pr * 2 + 1:pr * 2 + 2],
                    op0=ALU.mult, op1=ALU.add),
                    waits=[t_r3, t_q3, hts[h][1],
                           r1_tok.get(i - 1)], inc="sV")
            apply4_tok[i] = ap
            tr_done[("p4", i)] = transpose_chunk(i, ap, src=xb4)
            zw = [tr_done[("p4", i)][-1], t_ag, z_tok.get(i - 2),
                  bm_tok.get(JN - 2 + (i % 2))]
            mmz = None
            for k in range(KC):
                def fz(e, i=i, k=k):
                    return e.matmul(pk[i % 2][0:128, 0:36],
                                    fm[:, k * 1024 + i * 128:k * 1024 + i * 128 + 128],
                                    agb[:, k * 36:(k + 1) * 36],
                                    start=(k == 0), stop=(k == KC - 1))
                mmz = E("tensor", fz, waits=zw if k in (0, KC - 1) else (),
                        inc="sT" if k == KC - 1 else None)
            t_lg = E("vector", lambda e, i=i: e.tensor_add(
                lg[:, :], pk[i % 2][0:128, 32:36], ebias_b[:, :]),
                waits=[mmz, t_eb], inc="sV")
            def cc(o, a, b, op):
                return lambda e: e.tensor_tensor(cmp9[:, o:o + 1], lg[:, a:a + 1], lg[:, b:b + 1], op)
            specs = [(0, 0, 1, ALU.is_ge), (1, 0, 2, ALU.is_ge), (2, 0, 3, ALU.is_ge),
                     (3, 1, 0, ALU.is_gt), (4, 1, 2, ALU.is_ge), (5, 1, 3, ALU.is_ge),
                     (6, 2, 0, ALU.is_gt), (7, 2, 1, ALU.is_gt), (8, 2, 3, ALU.is_ge),
                     (9, 3, 0, ALU.is_gt), (10, 3, 1, ALU.is_gt), (11, 3, 2, ALU.is_gt)]
            for o, a, b, op in specs:
                E("vector", cc(o, a, b, op), inc="sV")
            for ei in range(4):
                E("vector", lambda e, ei=ei: e.tensor_mul(
                    ohm[:, ei:ei + 1], cmp9[:, ei * 3:ei * 3 + 1], cmp9[:, ei * 3 + 1:ei * 3 + 2]), inc="sV")
                E("vector", lambda e, ei=ei: e.tensor_mul(
                    ohm[:, ei:ei + 1], ohm[:, ei:ei + 1], cmp9[:, ei * 3 + 2:ei * 3 + 3]), inc="sV")
            t_zp = None
            for ei in range(4):
                t_zp = E("vector", lambda e, i=i, ei=ei: e.tensor_scalar(
                    out=zpb[:, ei * 8:(ei + 1) * 8], in0=pk[i % 2][0:128, ei * 8:(ei + 1) * 8],
                    scalar1=ohm[:, ei:ei + 1], scalar2=None, op0=ALU.mult), inc="sV")
            z_tok[i] = t_zp
            seq = tr_state["seq"]
            slot = pt[seq % 2]
            t_zt = E("tensor", lambda e, slot=slot: e.transpose(
                slot[0:32, 0:128], zpb[:, :], identb[:, :]),
                waits=[t_zp, tr_state["evtok"].get(seq - 2)], inc="sT")
            t_ze = E("vector", lambda e, i=i, slot=slot: e.tensor_copy(
                out=zfm[0:32, i * 128:(i + 1) * 128], in_=slot[0:32, 0:128]),
                waits=[t_zt], inc="sV")
            tr_state["evtok"][seq] = t_ze
            tr_state["seq"] += 1
            zfm_toks[i] = t_ze

        rq6_tok = {}
        tmp6_tok = {}

        def p6_post_i(i, t_ev):
            t_r4, t_q4, _ = stats_chain_acc("d", i, t_ev,
                                            r_ap=rq6[:, 2 * i:2 * i + 1],
                                            q_ap=rq6[:, 2 * i + 1:2 * i + 2])
            rq6_tok[i] = t_q4
            tmp6_tok[i] = E("vector", lambda e, i=i: e.tensor_scalar(
                out=mo[:, i * 4096:(i + 1) * 4096], in0=mo[:, i * 4096:(i + 1) * 4096],
                scalar1=rq6[:, 2 * i:2 * i + 1], scalar2=rq6[:, 2 * i + 1:2 * i + 2],
                op0=ALU.mult, op1=ALU.add),
                waits=[t_q4, t_ev], inc="sV")

        mo_free5 = {i: [r1_tok[i], tmp4_tok[i],
                        (tr_done.get(("p4", i + 1)) or [None])[-1],
                        apply4_tok.get(i + 1)]
                    for i in range(TC)}
        last_w0 = max(i for i in range(TC) if pmap4[i] == 0)
        wmb0_p4_free = [apply4_tok[last_w0], sq4_tok[last_w0], xm_tok[(last_w0, 0)],
                        xm_tok[(last_w0, 1)], xd4_tok[(last_w0, 0)], xd4_tok[(last_w0, 1)]]
        rec_weight_casts(0, i_bu, extra_waits=wmb0_p4_free)
        for j in range(JN):
            if j + 1 < JN:
                rec_weight_casts(j + 1, i_bu,
                                 extra_waits=[attn_last] if j + 1 == 1 else ())
            rec_mains(j, phase=5, t_bb=t_bb, zfm_toks=zfm_toks,
                      mo_free=mo_free5, acc_free=acc_read,
                      post_i_cb=p6_post_i if j == JN - 1 else None)

        y_toks = []
        y_tok = {}
        ap6 = {}
        ffn_last = main_tok[(JN - 1, TC - 1)]
        def fmq(k):
            return [(fmf[:, k * 4096:k * 4096 + HALF], None),
                    (fmf[:, k * 4096 + HALF:(k + 1) * 4096], None)]
        p6_slotmap = {0: xslots, 1: w0slots, 2: fmq(0), 3: fmq(1),
                      4: fmq(2), 5: fmq(3), 6: xslots, 7: w0slots}
        p6_sems = ["dX0", "dX1", "dX2", "dX3"]
        for i in range(TC):
            slots = p6_slotmap[i]
            if i == 0:
                xw = {h: [apply4_tok[TC - 1], sq4_tok[TC - 1], xm_tok[(TC - 1, h)]]
                      for h in range(2)}
            elif i == 1:
                xw = {h: [main_tok[(JN - 2, TC - 1)], apply4_tok[last_w0], sq4_tok[last_w0],
                          xm_tok[(last_w0, h)]] for h in range(2)}
            elif i in (6, 7):
                pv = i - 6
                xw = {h: [ap6[pv], y_tok[(pv, h)]] for h in range(2)}
            else:
                xw = {h: [ffn_last] for h in range(2)}
            td = {}
            for h in range(2):
                ap_slot = slots[h][0]
                semn = slots[h][1] or p6_sems[(2 * i + h) % 4]
                def fdx(e, i=i, h=h, ap_slot=ap_slot):
                    return e.dma_start(out=ap_slot,
                                      in_=d_xmid[i * 128:(i + 1) * 128, h * HALF:(h + 1) * HALF])
                td[h] = E("sync", fdx, waits=xw[h] + [xm_tok[(i, h)]],
                          inc=semn, amt=16)
            t_tmp = tmp6_tok[i]
            tos = {}
            for h in range(2):
                tos[h] = E("vector" if h == 0 else "gpsimd",
                    lambda e, i=i, h=h, ap_slot=slots[h][0]: e.tensor_add(
                    ap_slot, ap_slot, mo[:, i * 4096 + h * HALF:i * 4096 + (h + 1) * HALF]),
                    waits=[td[h], t_tmp], inc="sV" if h == 0 else "sP")
                y_tok[(i, h)] = E("scalar", lambda e, i=i, h=h, ap_slot=slots[h][0]: e.dma_start(
                    out=o_y[i * 128:(i + 1) * 128, h * HALF:(h + 1) * HALF], in_=ap_slot),
                    waits=[tos[h]], inc=store_sem(), amt=16)
                y_toks.append(y_tok[(i, h)])
            ap6[i] = tos[1]

        final_os = {}
        for tok in list(xm_tok.values()) + y_toks:
            sname = tok[0]
            final_os[sname] = max(final_os.get(sname, 0), tok[1])

        with nc.Block() as blk:
            def emit_stream(ename):
                def body(eng):
                    last = {}
                    for fn, wl, tok, amt in rec.streams[ename]:
                        for s, v in wl.items():
                            if tok is None and last.get(s, -1) >= v:
                                continue
                            eng.wait_ge(sems[s], v)
                            last[s] = max(last.get(s, -1), v)
                        ins = fn(eng)
                        if tok is not None:
                            ins.then_inc(sems[tok[0]], amt)
                    if ename == "sync":
                        for sname, v in sorted(final_os.items()):
                            eng.wait_ge(sems[sname], v)
                return body
            blk.sync(emit_stream("sync"))
            blk.tensor(emit_stream("tensor"))
            blk.vector(emit_stream("vector"))
            blk.scalar(emit_stream("scalar"))
            blk.gpsimd(emit_stream("gpsimd"))

    return nc


def _host_layout(inputs):
    x = np.asarray(inputs["L0_attn_data"], np.float32).reshape(8192, D)
    mask = np.ascontiguousarray(np.asarray(inputs["L0_attn_mask"], np.int32).T)
    WT = np.ascontiguousarray(np.asarray(inputs["L0_attn_weight"], np.float32).T)
    WU = np.ascontiguousarray(np.asarray(inputs["L0_ffn_W_up"], np.float32).T)

    def jtile(a):
        return np.ascontiguousarray(
            a.reshape(KC, 128, JN, NS).transpose(2, 1, 0, 3).reshape(JN, 128, KC * NS))

    mw = jtile(mask)
    bw = jtile(WT)
    bu = jtile(WU)
    lA = np.asarray(inputs["L0_ffn_lora_A"], np.float32).reshape(32, D)
    gw = np.asarray(inputs["L0_ffn_gate_w"], np.float32)
    AG = np.concatenate([lA, gw], axis=0)
    agh = np.ascontiguousarray(
        AG.T.reshape(KC, 128, 36).transpose(1, 0, 2).reshape(128, KC * 36))
    bb = np.ascontiguousarray(
        np.asarray(inputs["L0_ffn_lora_B"], np.float32).transpose(0, 2, 1).reshape(32, D))
    bias = np.asarray(inputs["L0_attn_bias"], np.float32).reshape(1, D)
    ebias = np.asarray(inputs["L0_ffn_expert_bias"], np.float32).reshape(1, 4)
    ident = np.eye(128, dtype=ml_dtypes.bfloat16)
    gmat = np.zeros((128, 128), dtype=ml_dtypes.bfloat16)
    for g in range(4):
        gmat[g * 32:(g + 1) * 32, g * 32:(g + 1) * 32] = 1
    onescol = np.ones((1, 128), dtype=ml_dtypes.bfloat16)

    shared = dict(mw=mw, bw=bw, bu=bu, agh=agh, bb=bb, bias=bias, ebias=ebias,
                  ident=ident, gmat=gmat, onescol=onescol)
    in_maps = []
    for c in range(NCORES):
        m = dict(shared)
        m["x"] = np.ascontiguousarray(x[c * T:(c + 1) * T])
        in_maps.append(m)
    return in_maps


def kernel(**inputs):
    in_maps = _host_layout(inputs)
    nc = build_program()
    res = run_bass_kernel_spmd(nc, in_maps, core_ids=list(range(NCORES)))
    out = np.concatenate([np.asarray(r["y"], np.float32) for r in res.results], axis=0)
    return np.ascontiguousarray(out.reshape(1, 8192, D))


# version 15
# speedup vs baseline: 1.0122x; 1.0122x over previous
import numpy as np
import ml_dtypes
from contextlib import ExitStack

import concourse.bass as bass
import concourse.mybir as mybir
from concourse.bass_utils import run_bass_kernel_spmd

F32 = mybir.dt.float32
BF16 = mybir.dt.bfloat16
I32 = mybir.dt.int32
AF = mybir.ActivationFunctionType
ALU = mybir.AluOpType
AX = mybir.AxisListType

D = 4096
T = 1024
TC = 8
KC = 32
JN = 16
NS = 256
HALF = 2048
NCORES = 8

ENGINES = ("sync", "tensor", "vector", "scalar", "gpsimd")


class Rec:

    COMPUTE_SEMS = ("sT", "sV", "sA", "sP")

    def __init__(self):
        self.streams = {e: [] for e in ENGINES}
        self.counts = {}
        self.last_compute = {e: None for e in ENGINES}
        self.last_dma = {}

    def emit(self, eng, fn, waits=(), inc=None, amt=1):
        wl = {}
        extra = [self.last_compute[eng]]
        if inc is not None and inc not in self.COMPUTE_SEMS:
            extra.append(self.last_dma.get(inc))
        for w in list(waits) + extra:
            if w is None:
                continue
            s, v, pe = w
            if wl.get(s, -1) < v:
                wl[s] = v
        tok = None
        if inc is not None:
            self.counts[inc] = self.counts.get(inc, 0) + amt
            tok = (inc, self.counts[inc], eng)
            if inc in self.COMPUTE_SEMS:
                self.last_compute[eng] = tok
            else:
                self.last_dma[inc] = tok
        self.streams[eng].append((fn, wl, tok, amt))
        return tok


def build_program():
    nc = bass.Bass("TRN2", target_bir_lowering=False)

    i_x = nc.dram_tensor("x", [T, D], F32, kind="ExternalInput").ap()
    i_mw = nc.dram_tensor("mw", [JN, 128, 8192], I32, kind="ExternalInput").ap()
    i_bw = nc.dram_tensor("bw", [JN, 128, 8192], F32, kind="ExternalInput").ap()
    i_bu = nc.dram_tensor("bu", [JN, 128, 8192], F32, kind="ExternalInput").ap()
    i_ag = nc.dram_tensor("agh", [128, 32 * 36], F32, kind="ExternalInput").ap()
    i_bb = nc.dram_tensor("bb", [32, D], F32, kind="ExternalInput").ap()
    i_bias = nc.dram_tensor("bias", [1, D], F32, kind="ExternalInput").ap()
    i_eb = nc.dram_tensor("ebias", [1, 4], F32, kind="ExternalInput").ap()
    i_id = nc.dram_tensor("ident", [128, 128], BF16, kind="ExternalInput").ap()
    i_g = nc.dram_tensor("gmat", [128, 128], BF16, kind="ExternalInput").ap()
    i_on = nc.dram_tensor("onescol", [1, 128], BF16, kind="ExternalInput").ap()
    d_xmid = nc.dram_tensor("xmid", [T, D], F32, kind="Internal").ap()
    o_y = nc.dram_tensor("y", [T, D], F32, kind="ExternalOutput").ap()

    rec = Rec()

    with ExitStack() as st:
        sb = lambda n, s, d: st.enter_context(nc.sbuf_tensor(n, s, d))
        psum = lambda n, s, d: st.enter_context(nc.psum_tensor(n, s, d))

        fm = sb("fm", [128, KC * 1024], BF16)
        mo = sb("mo", [128, TC * 4096], BF16)
        wmb = [sb("wmb0", [128, 8192], BF16),
               sb("wmb1", [128, 8192], BF16)]
        mring = [sb("mring0", [128, 1024], I32),
                 sb("mring1", [128, 1024], I32)]
        xst = [sb("xst0", [128, HALF], F32),
               sb("xst1", [128, HALF], F32)]
        xbst = sb("xbst", [128, D], BF16)
        junk = sb("junk", [128, HALF], BF16)
        agb = sb("agb", [128, 32 * 36], BF16)
        psj_i32 = sb("psj_i32", [128, 2 * 256], I32)
        psj_bf = sb("psj_bf", [128, 2 * 256], BF16)
        bmpj = sb("bmpj", [128, 2 * 256], BF16)
        zfm = sb("zfm", [32, TC * 128], BF16)
        zpb = sb("zpb", [128, 32], BF16)
        acc_s = sb("acc_s", [128, 128], F32)
        acc_q = sb("acc_q", [128, 128], F32)
        sums_h = sb("sums_h", [128, 4], F32)
        ss_h = sb("ss_h", [128, 4], F32)
        lg = sb("lg", [128, 4], F32)
        ohm = sb("ohm", [128, 4], F32)
        cmp9 = sb("cmp9", [128, 12], F32)
        ebias_b = sb("ebias_b", [128, 4], F32)
        identb = sb("identb", [128, 128], BF16)
        gmatb = sb("gmatb", [128, 128], BF16)
        onescol = sb("onescol_s", [1, 128], BF16)
        epsv = sb("epsv", [128, 1], F32)
        c_inv = sb("c_inv", [128, 1], F32)
        c_neg1 = sb("c_neg1", [128, 1], F32)
        c_zero = sb("c_zero", [128, 1], F32)
        c_one = sb("c_one", [128, 1], F32)
        stat = {nm: sb("st_" + nm, [128, 7], F32) for nm in ("a", "b", "c", "d")}
        rq4 = sb("rq4", [128, 16], F32)
        rq6 = sb("rq6", [128, 16], F32)
        rqP2 = sb("rqP2", [128, 4], F32)
        rqP4 = sb("rqP4", [128, 4], F32)

        pm = [psum(f"pm{b}", [128, NS], F32) for b in range(4)]
        pt = [psum(f"pt{b}", [128, 512], BF16) for b in range(2)]

        pk = [psum(f"pk{b}", [128, 256], F32) for b in range(2)]
        pm_tr = [p[:, :].bitcast(BF16) for p in pm]

        sems = {n: st.enter_context(nc.semaphore(n))
                for n in ("dC", "dM0", "dM1", "dW0", "dW1", "dX0", "dX1",
                          "dX2", "dX3", "dO0", "dO1", "dO2", "dO3",
                          "dP0", "dP1", "dP2", "dP3", "dP4", "dP5", "dP6", "dP7",
                          "dE", "dB", "sT", "sV", "sA", "sP")}

        bb_alias = [m[:, :].bitcast(BF16) for m in mring]
        bias_v = xst[0][0:1, :].bitcast(BF16)
        wmb0f = wmb[0][:, :].bitcast(F32)
        wmb1f = wmb[1][:, :].bitcast(F32)
        mof = mo[:, :].bitcast(F32)
        fmf = fm[:, :].bitcast(F32)
        xslots = [(xst[0][:, :], "dX0"), (xst[1][:, :], "dX1")]
        w0slots = [(wmb0f[:, 0:HALF], "dX2"), (wmb0f[:, HALF:2 * HALF], "dX3")]
        w1slots = [(wmb1f[:, 0:HALF], "dX2"), (wmb1f[:, HALF:2 * HALF], "dX3")]
        fm_v = fm[:, :].rearrange("p (kc t) -> p kc t", t=1024)

        E = rec.emit
        _store_ctr = [0]

        def store_sem():
            n = "dO%d" % (_store_ctr[0] % 4)
            _store_ctr[0] += 1
            return n

        E("sync", lambda e: e.dma_start(out=identb[:, :], in_=i_id[:, :]), inc="dC", amt=16)
        E("sync", lambda e: e.dma_start(out=gmatb[:, :], in_=i_g[:, :]), inc="dC", amt=16)
        t_consts = E("sync", lambda e: e.dma_start(out=onescol[:, :], in_=i_on[:, :]), inc="dC", amt=16)
        t_eb = E("scalar", lambda e: e.dma_start(out=ebias_b[:, :], in_=i_eb.broadcast_to([128, 4])), inc="dE", amt=16)
        t_ag = E("gpsimd", lambda e: e.dma_start(out=agb[:, :], in_=i_ag[:, :]), inc="dB", amt=16)
        E("vector", lambda e: e.memset(epsv[:, :], 1e-5), inc="sV")
        E("vector", lambda e: e.memset(c_inv[:, :], 1.0 / D), inc="sV")
        E("vector", lambda e: e.memset(c_neg1[:, :], -1.0), inc="sV")
        E("vector", lambda e: e.memset(c_zero[:, :], 0.0), inc="sV")
        t_const = E("vector", lambda e: e.memset(c_one[:, :], 1.0), inc="sV")

        def stats_chain(sk, t_sums, t_ss, sl=(0, 1), r_ap=None, q_ap=None):
            s = stat[sk]
            r_ap = r_ap if r_ap is not None else s[:, 5:6]
            q_ap = q_ap if q_ap is not None else s[:, 6:7]
            E("vector", lambda e: e.tensor_add(s[:, 1:2], sums_h[:, sl[0]:sl[0] + 1], sums_h[:, sl[1]:sl[1] + 1]),
              waits=[t_sums], inc="sV")
            E("vector", lambda e: e.tensor_scalar(s[:, 0:1], s[:, 1:2], c_inv[:, 0:1], None, ALU.mult), inc="sV")
            E("vector", lambda e: e.tensor_add(s[:, 2:3], ss_h[:, sl[0]:sl[0] + 1], ss_h[:, sl[1]:sl[1] + 1]),
              waits=[t_ss], inc="sV")
            E("vector", lambda e: e.tensor_scalar(s[:, 2:3], s[:, 2:3], c_inv[:, 0:1], None, ALU.mult), inc="sV")
            E("vector", lambda e: e.tensor_mul(s[:, 3:4], s[:, 0:1], s[:, 0:1]), inc="sV")
            t_var = E("vector", lambda e: e.tensor_sub(s[:, 3:4], s[:, 2:3], s[:, 3:4]), inc="sV")
            t_sd = E("scalar", lambda e: e.activation(s[:, 4:5], s[:, 3:4], AF.Sqrt,
                                                      bias=epsv[:, 0:1], scale=1.0),
                     waits=[t_var, t_const], inc="sA")
            t_r = E("vector", lambda e, r_ap=r_ap: e.reciprocal(r_ap, s[:, 4:5]), waits=[t_sd], inc="sV")
            E("vector", lambda e, r_ap=r_ap: e.tensor_mul(s[:, 6:7], s[:, 0:1], r_ap), inc="sV")
            t_q = E("vector", lambda e, q_ap=q_ap: e.tensor_scalar(q_ap, s[:, 6:7], c_neg1[:, 0:1], None, ALU.mult), inc="sV")
            return t_r, t_q

        def stats_chain_acc(sk, i, t_ready, r_ap=None, q_ap=None):
            s = stat[sk]
            r_ap = r_ap if r_ap is not None else s[:, 5:6]
            q_ap = q_ap if q_ap is not None else s[:, 6:7]
            t1 = E("vector", lambda e: e.reduce_sum(s[:, 1:2], acc_s[:, i * 16:(i + 1) * 16], axis=AX.X),
                   waits=[t_ready], inc="sV")
            E("vector", lambda e: e.tensor_scalar(s[:, 0:1], s[:, 1:2], c_inv[:, 0:1], None, ALU.mult), inc="sV")
            t2 = E("vector", lambda e: e.reduce_sum(s[:, 2:3], acc_q[:, i * 16:(i + 1) * 16], axis=AX.X),
                   waits=[t_ready], inc="sV")
            E("vector", lambda e: e.tensor_scalar(s[:, 2:3], s[:, 2:3], c_inv[:, 0:1], None, ALU.mult), inc="sV")
            E("vector", lambda e: e.tensor_mul(s[:, 3:4], s[:, 0:1], s[:, 0:1]), inc="sV")
            t_var = E("vector", lambda e: e.tensor_sub(s[:, 3:4], s[:, 2:3], s[:, 3:4]), inc="sV")
            t_sd = E("scalar", lambda e: e.activation(s[:, 4:5], s[:, 3:4], AF.Sqrt,
                                                      bias=epsv[:, 0:1], scale=1.0),
                     waits=[t_var], inc="sA")
            t_r = E("vector", lambda e, r_ap=r_ap: e.reciprocal(r_ap, s[:, 4:5]), waits=[t_sd], inc="sV")
            E("vector", lambda e, r_ap=r_ap: e.tensor_mul(s[:, 6:7], s[:, 0:1], r_ap), inc="sV")
            t_q = E("vector", lambda e, q_ap=q_ap: e.tensor_scalar(q_ap, s[:, 6:7], c_neg1[:, 0:1], None, ALU.mult), inc="sV")
            return t_r, t_q, (t1, t2)

        tr_state = {"seq": 0, "evtok": {}}

        def transpose_chunk(i, t_apply, src=None, evac_eng="scalar", slots_l=None, extra_w=()):
            src = src if src is not None else xbst
            slots_l = slots_l or pt
            ns = len(slots_l)
            toks = []
            for g in range(8):
                seq = tr_state["seq"]
                slot = slots_l[seq % ns]
                w = [t_apply, tr_state["evtok"].get(seq - ns), t_consts] + list(extra_w)
                t_tr = None
                for u in range(4):
                    kc = g * 4 + u
                    def f(e, kc=kc, u=u, slot=slot, src=src):
                        return e.transpose(slot[0:128, u * 128:(u + 1) * 128],
                                           src[:, kc * 128:(kc + 1) * 128], identb[:, :])
                    t_tr = E("tensor", f, waits=w if u in (0, 3) else (),
                             inc="sT" if u == 3 else None)
                def fe(e, g=g, i=i, slot=slot):
                    fn = e.copy if hasattr(e, "copy") else e.tensor_copy
                    return fn(
                        out=fm_v[:, g * 4:(g + 1) * 4, i * 128:(i + 1) * 128],
                        in_=slot[0:128, 0:512].rearrange("p (a b) -> p a b", b=128))
                t_ev = E(evac_eng, fe, waits=[t_tr],
                         inc="sA" if evac_eng == "scalar" else "sV")
                tr_state["evtok"][seq] = t_ev
                tr_state["seq"] += 1
                toks.append(t_ev)
            return toks

        apply_tok = {}
        sqh_tok = {}
        xdma_tok = {}
        tr_done = {}
        def moq(c):
            return [(mof[:, c * 4096:c * 4096 + HALF], None),
                    (mof[:, c * 4096 + HALF:(c + 1) * 4096], None)]
        p2_slotmap = {0: moq(0), 1: moq(1), 2: moq(2), 3: moq(3),
                      4: xslots, 5: w1slots, 6: moq(0), 7: moq(1)}
        p2_sems = ["dP0", "dP1", "dP2", "dP3", "dP4", "dP5", "dP6", "dP7"]

        def rec_p2_chunk(c):
            slots = p2_slotmap[c]
            sk = "a" if c % 2 == 0 else "b"
            pr = c % 2
            half_toks = []
            for h in range(2):
                ap_slot = slots[h][0]
                semn = p2_sems[(2 * c + h) % 8]
                sl = pr * 2 + h
                w = ([apply_tok[c - 6], sqh_tok[(c - 6, h)]] if c >= 6 else [])
                def fd(e, c=c, h=h, ap_slot=ap_slot):
                    return e.dma_start(out=ap_slot,
                                      in_=i_x[c * 128:(c + 1) * 128, h * HALF:(h + 1) * HALF])
                t_d = E("gpsimd", fd, waits=w, inc=semn, amt=16)
                xdma_tok[(c, h)] = t_d
                def fs(e, ap_slot=ap_slot, sl=sl):
                    return e.reduce_sum(sums_h[:, sl:sl + 1], ap_slot, axis=AX.X)
                t_s = E("vector", fs, waits=[t_d], inc="sV")
                def fq(e, ap_slot=ap_slot, sl=sl):
                    return e.activation(junk[:, :], ap_slot,
                                        AF.Square, accum_out=ss_h[:, sl:sl + 1])
                t_q = E("scalar", fq, waits=[t_d, apply_tok.get(c - 2)], inc="sA")
                sqh_tok[(c, h)] = t_q
                half_toks.append((t_s, t_q))
            t_r, t_qq = stats_chain(sk, half_toks[1][0], half_toks[1][1],
                                    sl=(pr * 2, pr * 2 + 1),
                                    r_ap=rqP2[:, pr * 2:pr * 2 + 1],
                                    q_ap=rqP2[:, pr * 2 + 1:pr * 2 + 2])
            ap = None
            for h in range(2):
                def fa(e, h=h, pr=pr, ap_slot=slots[h][0]):
                    return e.tensor_scalar(
                        out=xbst[:, h * HALF:(h + 1) * HALF],
                        in0=ap_slot,
                        scalar1=rqP2[:, pr * 2:pr * 2 + 1],
                        scalar2=rqP2[:, pr * 2 + 1:pr * 2 + 2],
                        op0=ALU.mult, op1=ALU.add)
                ap = E("vector", fa, waits=[t_r, t_qq, half_toks[h][1],
                                            (tr_done.get(c - 1) or [None])[-1]], inc="sV")
            apply_tok[c] = ap
            tr_done[c] = transpose_chunk(c, ap, evac_eng="vector")

        seg_tok = {}
        cast_half_tok = {}
        cvt_tok = {}
        g_tok = {}
        bm_tok = {}
        wm_tok = {}
        cast_tok = {}
        main_tok = {}
        ev_tok = {}
        last_seg = {"t": None}
        last_mdma = {}

        def rec_mask_pieces(j, rng=range(8)):
            for pc in rng:
                idx = j * 8 + pc
                def fd(e, j=j, pc=pc, idx=idx):
                    return e.dma_start(out=mring[idx % 2][:, :],
                                      in_=i_mw[j, :, pc * 1024:(pc + 1) * 1024])
                t_d = E("sync", fd, waits=[seg_tok.get(((idx - 2) // 8, (idx - 2) % 8))],
                        inc="dM%d" % (idx % 2), amt=16)
                def fs(e, j=j, pc=pc, idx=idx):
                    with nc.allow_low_precision(reason="int32 block-sum of mask ints is exact"):
                        return e.tensor_reduce(
                            out=psj_i32[:, (j % 2) * 256 + pc * 32:(j % 2) * 256 + (pc + 1) * 32],
                            in_=mring[idx % 2][:, :].rearrange("p (g w) -> p g w", w=32),
                            axis=AX.X, op=ALU.add)
                t_s = E("vector", fs, waits=[t_d, cvt_tok.get(j - 2)], inc="sV")
                seg_tok[(j, pc)] = t_s
                last_seg["t"] = t_s
                last_mdma[idx % 2] = t_d

        def rec_mask_cvt(j):
            def fc(e, j=j):
                return e.tensor_copy(out=psj_bf[:, (j % 2) * 256:(j % 2 + 1) * 256],
                                     in_=psj_i32[:, (j % 2) * 256:(j % 2 + 1) * 256])
            cvt_tok[j] = E("vector", fc, waits=[seg_tok[(j, 7)], g_tok.get(j - 2)], inc="sV")

        def rec_mask_gbm(j):
            def fg(e, j=j):
                return e.matmul(pk[j % 2][0:128, 0:256], gmatb[:, :],
                                psj_bf[:, (j % 2) * 256:(j % 2 + 1) * 256],
                                start=True, stop=True)
            t_g = E("tensor", fg, waits=[cvt_tok[j], t_consts, bm_tok.get(j - 2)], inc="sT")
            g_tok[j] = t_g
            def fb(e, j=j):
                return e.tensor_single_scalar(out=bmpj[:, (j % 2) * 256:(j % 2 + 1) * 256],
                                              in_=pk[j % 2][0:128, 0:256],
                                              scalar=c_zero[:, 0:1], op=ALU.is_gt)
            bm_tok[j] = E("vector", fb, waits=[t_g, t_const, wm_tok.get(j - 2)], inc="sV")

        def rec_weight_casts(j, hbm, extra_waits=()):
            toks = []
            for hh in range(2):
                def fd(e, j=j, hh=hh, hbm=hbm):
                    return e.dma_start(out=wmb[j % 2][:, hh * 4096:(hh + 1) * 4096],
                                      in_=hbm[j, :, hh * 4096:(hh + 1) * 4096])
                toks.append(E("gpsimd", fd,
                              waits=[main_tok.get((j - 2, TC - 1)),
                                     wm_tok.get(j - 2)] + list(extra_waits) if hh == 0 else [],
                              inc="dW%d" % (j % 2), amt=16))
            cast_tok[j] = toks[1]
            cast_half_tok[j] = toks

        def rec_mask_tt_half(j, hh):
            def ft(e, j=j, hh=hh):
                return e.tensor_tensor(
                    out=wmb[j % 2][:, hh * 4096:(hh + 1) * 4096]
                        .rearrange("p (kc nb w) -> p kc nb w", kc=16, w=32),
                    in0=wmb[j % 2][:, hh * 4096:(hh + 1) * 4096]
                        .rearrange("p (kc nb w) -> p kc nb w", kc=16, w=32),
                    in1=bmpj[:, (j % 2) * 256 + hh * 128:(j % 2) * 256 + (hh + 1) * 128]
                        .rearrange("p (kc nb) -> p kc nb", kc=16)
                        .unsqueeze(3).broadcast_to([128, 16, 8, 32]),
                    op=ALU.mult)
            tok = E("vector", ft, waits=[cast_half_tok[j][hh], bm_tok[j]], inc="sV")
            if hh == 1:
                wm_tok[j] = tok
            return tok

        def rec_mains(j, phase, t_bias=None, t_bb=None, zfm_toks=None, mo_free=None,
                      acc_free=None, post_i_cb=None):
            for i in range(TC):
                seq = (0 if phase == 3 else JN * TC) + j * TC + i
                bank = pm[seq % 4]
                w = [wm_tok.get(j) if phase == 3 else cast_tok[j],
                     tr_done[i][-1] if phase == 3 else None,
                     (zfm_toks[TC - 1] if (phase == 5 and j == 0) else None),
                     ev_tok.get(seq - 4),
                     t_bias, t_bb,
                     zfm_toks[i] if zfm_toks else None,
                     t_consts]
                mm = None
                for k in range(KC):
                    def fmm(e, j=j, i=i, k=k, bank=bank):
                        return e.matmul(bank[0:128, 0:NS],
                                        fm[:, k * 1024 + i * 128:k * 1024 + i * 128 + 128],
                                        wmb[j % 2][:, k * NS:(k + 1) * NS],
                                        start=(k == 0), stop=False)
                    E("tensor", fmm, waits=w if k == 0 else ())
                if phase == 3:
                    def fr1(e, j=j, bank=bank):
                        return e.matmul(bank[0:128, 0:NS], onescol[0:1, 0:128],
                                        bias_v[0:1, j * NS:(j + 1) * NS],
                                        start=False, stop=True)
                    t_m = E("tensor", fr1, waits=w, inc="sT")
                else:
                    def fdl(e, j=j, i=i, bank=bank):
                        ba = bb_alias[j // 8]
                        return e.matmul(bank[0:128, 0:NS], zfm[0:32, i * 128:(i + 1) * 128],
                                        ba[0:32, (j % 8) * NS:(j % 8 + 1) * NS],
                                        start=False, stop=True)
                    t_m = E("tensor", fdl, waits=w, inc="sT")
                main_tok[(j, i)] = t_m
                def fe1(e, j=j, i=i, bank=bank):
                    return e.activation(out=mo[:, i * 4096 + j * NS:i * 4096 + (j + 1) * NS],
                                        in_=bank[0:128, 0:NS], func=AF.Identity,
                                        accum_out=acc_s[:, i * 16 + j:i * 16 + j + 1])
                mw = mo_free[i] if mo_free else None
                mw = mw if isinstance(mw, list) else [mw]
                E("scalar", fe1, waits=[t_m, acc_free[i] if acc_free else None] + mw,
                  inc="sA")
                def fe2(e, j=j, i=i, bank=bank):
                    return e.activation(out=junk[:, 0:NS], in_=bank[0:128, 0:NS],
                                        func=AF.Square,
                                        accum_out=acc_q[:, i * 16 + j:i * 16 + j + 1])
                ev_tok[seq] = E("scalar", fe2, inc="sA")
                if post_i_cb is not None:
                    post_i_cb(i, ev_tok[seq])

        rec_weight_casts(0, i_bw)
        rec_p2_chunk(0)
        rec_p2_chunk(1)
        rec_mask_pieces(0); rec_mask_cvt(0)
        rec_p2_chunk(2)
        rec_mask_gbm(0)
        rec_mask_tt_half(0, 0); rec_mask_tt_half(0, 1)
        rec_p2_chunk(3)
        rec_mask_pieces(1); rec_mask_cvt(1)
        rec_p2_chunk(4)
        rec_p2_chunk(5)
        rec_mask_gbm(1)
        rec_p2_chunk(6)
        rec_p2_chunk(7)
        fm_ready = tr_done[TC - 1][-1]
        def p2_guard(i):
            cs = [i // 2] + ([6] if i // 2 == 0 else ([7] if i // 2 == 1 else []))
            g = []
            for c in cs:
                g += [apply_tok[c], sqh_tok[(c, 0)], sqh_tok[(c, 1)],
                      xdma_tok[(c, 0)], xdma_tok[(c, 1)]]
            return g
        mo_guard = {i: p2_guard(i) for i in range(TC)}
        t_bias = E("gpsimd", lambda e: e.dma_start(out=bias_v[0:1, :], in_=i_bias[0:1, :]),
                   waits=[apply_tok[4], xdma_tok[(4, 0)],
                          sqh_tok[(4, 0)]], inc="dB", amt=16)
        wmb1_p2_free = [apply_tok[5], sqh_tok[(5, 0)], sqh_tok[(5, 1)],
                        xdma_tok[(5, 0)], xdma_tok[(5, 1)]]
        acc_read = {}
        rq4_tok = {}
        tmp4_tok = {}

        def p4_post_i(i, t_ev):
            t_r2, t_q2, _rd = stats_chain_acc("b", i, t_ev,
                                              r_ap=rq4[:, 2 * i:2 * i + 1],
                                              q_ap=rq4[:, 2 * i + 1:2 * i + 2])
            acc_read[i] = _rd[1]
            rq4_tok[i] = t_q2
            tmp4_tok[i] = E("vector", lambda e, i=i: e.tensor_scalar(
                out=mo[:, i * 4096:(i + 1) * 4096], in0=mo[:, i * 4096:(i + 1) * 4096],
                scalar1=rq4[:, 2 * i:2 * i + 1], scalar2=rq4[:, 2 * i + 1:2 * i + 2],
                op0=ALU.mult, op1=ALU.add),
                waits=[t_q2, t_ev], inc="sV")

        for j in range(JN):
            if j + 1 < JN:
                rec_weight_casts(j + 1, i_bw,
                                 extra_waits=wmb1_p2_free if j + 1 == 1 else ())
            if j + 2 < JN:
                rec_mask_pieces(j + 2, range(0, 4))
                rec_mask_tt_half(j + 1, 0)
                rec_mask_pieces(j + 2, range(4, 8))
                rec_mask_tt_half(j + 1, 1)
                rec_mask_cvt(j + 2)
            elif j + 1 < JN:
                rec_mask_tt_half(j + 1, 0)
                rec_mask_tt_half(j + 1, 1)
            rec_mains(j, phase=3, t_bias=t_bias,
                      mo_free=mo_guard if j == 0 else None,
                      post_i_cb=p4_post_i if j == JN - 1 else None)
            if j + 2 < JN:
                rec_mask_gbm(j + 2)
        attn_last = main_tok[(JN - 1, TC - 1)]

        E("gpsimd", lambda e: e.dma_start(out=bb_alias[0][0:32, :], in_=i_bb[:, 0:2048]),
          waits=[last_seg["t"], last_mdma.get(0)], inc="dB", amt=16)
        t_bb = E("gpsimd", lambda e: e.dma_start(out=bb_alias[1][0:32, :], in_=i_bb[:, 2048:4096]),
                 waits=[last_seg["t"], last_mdma.get(1)], inc="dB", amt=16)
        zfm_toks = {}
        r1_tok = {}
        apply4_tok = {}
        xm_tok = {}
        xd4_tok = {}
        z_tok = {}
        sq4_tok = {}
        pmap4 = [(i % 2) if i < TC - 2 else 1 for i in range(TC)]
        if TC >= 2:
            pmap4[0] = 0
        pairs4 = [w0slots, xslots]
        prev_same4 = {}
        lastu = {}
        for i in range(TC):
            prev_same4[i] = lastu.get(pmap4[i])
            lastu[pmap4[i]] = i
        for i in range(TC):
            pr = pmap4[i]
            slots = pairs4[pr]
            t_tmp = tmp4_tok[i]
            pv = prev_same4[i]
            if pv is None:
                xw0 = ([main_tok[(JN - 2, TC - 1)], wm_tok.get(JN - 2)] if pr == 0
                       else [attn_last, t_bias])
                xw = {0: xw0, 1: xw0}
            else:
                xw = {h: [apply4_tok[pv], sq4_tok[pv], xm_tok[(pv, h)]] for h in range(2)}
            td = {}
            for h in range(2):
                def fdx(e, i=i, h=h, ap_slot=slots[h][0]):
                    return e.dma_start(out=ap_slot,
                                      in_=i_x[i * 128:(i + 1) * 128, h * HALF:(h + 1) * HALF])
                td[h] = E("sync", fdx, waits=xw[h], inc=slots[h][1], amt=16)
                xd4_tok[(i, h)] = td[h]
            hts = []
            trh = {}
            for h in range(2):
                sl = pr * 2 + h
                def fr(e, i=i, h=h, ap_slot=slots[h][0], sl=sl):
                    return e.scalar_tensor_tensor(
                        out=ap_slot, in0=mo[:, i * 4096 + h * HALF:i * 4096 + (h + 1) * HALF],
                        scalar=c_one[:, 0:1], in1=ap_slot,
                        op0=ALU.mult, op1=ALU.add, accum_out=sums_h[:, sl:sl + 1])
                trh[h] = E("vector", fr, waits=[td[h], t_tmp, t_const], inc="sV")
                xm_tok[(i, h)] = E("scalar", lambda e, i=i, h=h, ap_slot=slots[h][0]: e.dma_start(
                    out=d_xmid[i * 128:(i + 1) * 128, h * HALF:(h + 1) * HALF], in_=ap_slot),
                    waits=[trh[h]], inc=store_sem(), amt=16)
                tq = E("scalar", lambda e, ap_slot=slots[h][0], sl=sl: e.activation(
                    junk[:, :], ap_slot, AF.Square,
                    accum_out=ss_h[:, sl:sl + 1]),
                    waits=[trh[h], apply4_tok.get(prev_same4[i])], inc="sA")
                hts.append((trh[h], tq))
            r1_tok[i] = trh[1]
            sq4_tok[i] = hts[1][1]
            t_r3, t_q3 = stats_chain("c" if pr == 0 else "d",
                                     hts[1][0], hts[1][1], sl=(pr * 2, pr * 2 + 1),
                                     r_ap=rqP4[:, pr * 2:pr * 2 + 1],
                                     q_ap=rqP4[:, pr * 2 + 1:pr * 2 + 2])
            xb4 = xbst if i == 0 else mo[:, (i - 1) * 4096:i * 4096]
            ap = None
            for h in range(2):
                ap = E("vector", lambda e, h=h, pr=pr, xb4=xb4, ap_slot=slots[h][0]: e.tensor_scalar(
                    out=xb4[:, h * HALF:(h + 1) * HALF], in0=ap_slot,
                    scalar1=rqP4[:, pr * 2:pr * 2 + 1],
                    scalar2=rqP4[:, pr * 2 + 1:pr * 2 + 2],
                    op0=ALU.mult, op1=ALU.add),
                    waits=[t_r3, t_q3, hts[h][1],
                           r1_tok.get(i - 1)], inc="sV")
            apply4_tok[i] = ap
            tr_done[("p4", i)] = transpose_chunk(
                i, ap, src=xb4, slots_l=pt + pm_tr,
                extra_w=[ev_tok[JN * TC - 1], tr_done[TC - 1][-1]])
            zw = [tr_done[("p4", i)][-1], t_ag, z_tok.get(i - 2),
                  bm_tok.get(JN - 2 + (i % 2))]
            mmz = None
            for k in range(KC):
                def fz(e, i=i, k=k):
                    return e.matmul(pk[i % 2][0:128, 0:36],
                                    fm[:, k * 1024 + i * 128:k * 1024 + i * 128 + 128],
                                    agb[:, k * 36:(k + 1) * 36],
                                    start=(k == 0), stop=(k == KC - 1))
                mmz = E("tensor", fz, waits=zw if k in (0, KC - 1) else (),
                        inc="sT" if k == KC - 1 else None)
            t_lg = E("vector", lambda e, i=i: e.tensor_add(
                lg[:, :], pk[i % 2][0:128, 32:36], ebias_b[:, :]),
                waits=[mmz, t_eb], inc="sV")
            def cc(o, a, b, op):
                return lambda e: e.tensor_tensor(cmp9[:, o:o + 1], lg[:, a:a + 1], lg[:, b:b + 1], op)
            specs = [(0, 0, 1, ALU.is_ge), (1, 0, 2, ALU.is_ge), (2, 0, 3, ALU.is_ge),
                     (3, 1, 0, ALU.is_gt), (4, 1, 2, ALU.is_ge), (5, 1, 3, ALU.is_ge),
                     (6, 2, 0, ALU.is_gt), (7, 2, 1, ALU.is_gt), (8, 2, 3, ALU.is_ge),
                     (9, 3, 0, ALU.is_gt), (10, 3, 1, ALU.is_gt), (11, 3, 2, ALU.is_gt)]
            for o, a, b, op in specs:
                E("vector", cc(o, a, b, op), inc="sV")
            for ei in range(4):
                E("vector", lambda e, ei=ei: e.tensor_mul(
                    ohm[:, ei:ei + 1], cmp9[:, ei * 3:ei * 3 + 1], cmp9[:, ei * 3 + 1:ei * 3 + 2]), inc="sV")
                E("vector", lambda e, ei=ei: e.tensor_mul(
                    ohm[:, ei:ei + 1], ohm[:, ei:ei + 1], cmp9[:, ei * 3 + 2:ei * 3 + 3]), inc="sV")
            t_zp = None
            for ei in range(4):
                t_zp = E("vector", lambda e, i=i, ei=ei: e.tensor_scalar(
                    out=zpb[:, ei * 8:(ei + 1) * 8], in0=pk[i % 2][0:128, ei * 8:(ei + 1) * 8],
                    scalar1=ohm[:, ei:ei + 1], scalar2=None, op0=ALU.mult), inc="sV")
            z_tok[i] = t_zp
            seq = tr_state["seq"]
            slot = (pt + pm_tr)[seq % 6]
            t_zt = E("tensor", lambda e, slot=slot: e.transpose(
                slot[0:32, 0:128], zpb[:, :], identb[:, :]),
                waits=[t_zp, tr_state["evtok"].get(seq - 6),
                       ev_tok[JN * TC - 1]], inc="sT")
            t_ze = E("vector", lambda e, i=i, slot=slot: e.tensor_copy(
                out=zfm[0:32, i * 128:(i + 1) * 128], in_=slot[0:32, 0:128]),
                waits=[t_zt], inc="sV")
            tr_state["evtok"][seq] = t_ze
            tr_state["seq"] += 1
            zfm_toks[i] = t_ze

        rq6_tok = {}
        tmp6_tok = {}

        def p6_post_i(i, t_ev):
            t_r4, t_q4, _ = stats_chain_acc("d", i, t_ev,
                                            r_ap=rq6[:, 2 * i:2 * i + 1],
                                            q_ap=rq6[:, 2 * i + 1:2 * i + 2])
            rq6_tok[i] = t_q4
            tmp6_tok[i] = E("vector", lambda e, i=i: e.tensor_scalar(
                out=mo[:, i * 4096:(i + 1) * 4096], in0=mo[:, i * 4096:(i + 1) * 4096],
                scalar1=rq6[:, 2 * i:2 * i + 1], scalar2=rq6[:, 2 * i + 1:2 * i + 2],
                op0=ALU.mult, op1=ALU.add),
                waits=[t_q4, t_ev], inc="sV")

        mo_free5 = {i: [r1_tok[i], tmp4_tok[i],
                        (tr_done.get(("p4", i + 1)) or [None])[-1],
                        apply4_tok.get(i + 1)]
                    for i in range(TC)}
        last_w0 = max(i for i in range(TC) if pmap4[i] == 0)
        wmb0_p4_free = [apply4_tok[last_w0], sq4_tok[last_w0], xm_tok[(last_w0, 0)],
                        xm_tok[(last_w0, 1)], xd4_tok[(last_w0, 0)], xd4_tok[(last_w0, 1)]]
        rec_weight_casts(0, i_bu, extra_waits=wmb0_p4_free)
        for j in range(JN):
            if j + 1 < JN:
                rec_weight_casts(j + 1, i_bu,
                                 extra_waits=[attn_last] if j + 1 == 1 else ())
            rec_mains(j, phase=5, t_bb=t_bb, zfm_toks=zfm_toks,
                      mo_free=mo_free5, acc_free=acc_read,
                      post_i_cb=p6_post_i if j == JN - 1 else None)

        y_toks = []
        y_tok = {}
        ap6 = {}
        ffn_last = main_tok[(JN - 1, TC - 1)]
        def fmq(k):
            return [(fmf[:, k * 4096:k * 4096 + HALF], None),
                    (fmf[:, k * 4096 + HALF:(k + 1) * 4096], None)]
        p6_slotmap = {0: xslots, 1: w0slots, 2: fmq(0), 3: fmq(1),
                      4: fmq(2), 5: fmq(3), 6: xslots, 7: w0slots}
        p6_sems = ["dX0", "dX1", "dX2", "dX3"]
        for i in range(TC):
            slots = p6_slotmap[i]
            if i == 0:
                xw = {h: [apply4_tok[TC - 1], sq4_tok[TC - 1], xm_tok[(TC - 1, h)]]
                      for h in range(2)}
            elif i == 1:
                xw = {h: [main_tok[(JN - 2, TC - 1)], apply4_tok[last_w0], sq4_tok[last_w0],
                          xm_tok[(last_w0, h)]] for h in range(2)}
            elif i in (6, 7):
                pv = i - 6
                xw = {h: [ap6[pv], y_tok[(pv, h)]] for h in range(2)}
            else:
                xw = {h: [ffn_last] for h in range(2)}
            td = {}
            for h in range(2):
                ap_slot = slots[h][0]
                semn = slots[h][1] or p6_sems[(2 * i + h) % 4]
                def fdx(e, i=i, h=h, ap_slot=ap_slot):
                    return e.dma_start(out=ap_slot,
                                      in_=d_xmid[i * 128:(i + 1) * 128, h * HALF:(h + 1) * HALF])
                td[h] = E("sync", fdx, waits=xw[h] + [xm_tok[(i, h)]],
                          inc=semn, amt=16)
            t_tmp = tmp6_tok[i]
            tos = {}
            for h in range(2):
                tos[h] = E("vector" if h == 0 else "gpsimd",
                    lambda e, i=i, h=h, ap_slot=slots[h][0]: e.tensor_add(
                    ap_slot, ap_slot, mo[:, i * 4096 + h * HALF:i * 4096 + (h + 1) * HALF]),
                    waits=[td[h], t_tmp], inc="sV" if h == 0 else "sP")
                y_tok[(i, h)] = E("scalar", lambda e, i=i, h=h, ap_slot=slots[h][0]: e.dma_start(
                    out=o_y[i * 128:(i + 1) * 128, h * HALF:(h + 1) * HALF], in_=ap_slot),
                    waits=[tos[h]], inc=store_sem(), amt=16)
                y_toks.append(y_tok[(i, h)])
            ap6[i] = tos[1]

        final_os = {}
        for tok in list(xm_tok.values()) + y_toks:
            sname = tok[0]
            final_os[sname] = max(final_os.get(sname, 0), tok[1])

        with nc.Block() as blk:
            def emit_stream(ename):
                def body(eng):
                    last = {}
                    for fn, wl, tok, amt in rec.streams[ename]:
                        for s, v in wl.items():
                            if tok is None and last.get(s, -1) >= v:
                                continue
                            eng.wait_ge(sems[s], v)
                            last[s] = max(last.get(s, -1), v)
                        ins = fn(eng)
                        if tok is not None:
                            ins.then_inc(sems[tok[0]], amt)
                    if ename == "sync":
                        for sname, v in sorted(final_os.items()):
                            eng.wait_ge(sems[sname], v)
                return body
            blk.sync(emit_stream("sync"))
            blk.tensor(emit_stream("tensor"))
            blk.vector(emit_stream("vector"))
            blk.scalar(emit_stream("scalar"))
            blk.gpsimd(emit_stream("gpsimd"))

    return nc


def _host_layout(inputs):
    x = np.asarray(inputs["L0_attn_data"], np.float32).reshape(8192, D)
    mask = np.ascontiguousarray(np.asarray(inputs["L0_attn_mask"], np.int32).T)
    WT = np.ascontiguousarray(np.asarray(inputs["L0_attn_weight"], np.float32).T)
    WU = np.ascontiguousarray(np.asarray(inputs["L0_ffn_W_up"], np.float32).T)

    def jtile(a):
        return np.ascontiguousarray(
            a.reshape(KC, 128, JN, NS).transpose(2, 1, 0, 3).reshape(JN, 128, KC * NS))

    mw = jtile(mask)
    bw = jtile(WT)
    bu = jtile(WU)
    lA = np.asarray(inputs["L0_ffn_lora_A"], np.float32).reshape(32, D)
    gw = np.asarray(inputs["L0_ffn_gate_w"], np.float32)
    AG = np.concatenate([lA, gw], axis=0)
    agh = np.ascontiguousarray(
        AG.T.reshape(KC, 128, 36).transpose(1, 0, 2).reshape(128, KC * 36))
    bb = np.ascontiguousarray(
        np.asarray(inputs["L0_ffn_lora_B"], np.float32).transpose(0, 2, 1).reshape(32, D))
    bias = np.asarray(inputs["L0_attn_bias"], np.float32).reshape(1, D)
    ebias = np.asarray(inputs["L0_ffn_expert_bias"], np.float32).reshape(1, 4)
    ident = np.eye(128, dtype=ml_dtypes.bfloat16)
    gmat = np.zeros((128, 128), dtype=ml_dtypes.bfloat16)
    for g in range(4):
        gmat[g * 32:(g + 1) * 32, g * 32:(g + 1) * 32] = 1
    onescol = np.ones((1, 128), dtype=ml_dtypes.bfloat16)

    shared = dict(mw=mw, bw=bw, bu=bu, agh=agh, bb=bb, bias=bias, ebias=ebias,
                  ident=ident, gmat=gmat, onescol=onescol)
    in_maps = []
    for c in range(NCORES):
        m = dict(shared)
        m["x"] = np.ascontiguousarray(x[c * T:(c + 1) * T])
        in_maps.append(m)
    return in_maps


def kernel(**inputs):
    in_maps = _host_layout(inputs)
    nc = build_program()
    res = run_bass_kernel_spmd(nc, in_maps, core_ids=list(range(NCORES)))
    out = np.concatenate([np.asarray(r["y"], np.float32) for r in res.results], axis=0)
    return np.ascontiguousarray(out.reshape(1, 8192, D))
